# version 17
# baseline (speedup 1.0000x reference)
import contextlib
import numpy as np
import ml_dtypes

import concourse.bass as bass
import concourse.mybir as mybir
from concourse.bass_utils import run_bass_kernel_spmd
from concourse.alu_op_type import AluOpType as ALU

F32 = mybir.dt.float32
BF16 = mybir.dt.bfloat16
AF = mybir.ActivationFunctionType
AX = mybir.AxisListType.X
BF = ml_dtypes.bfloat16

NCORES = 8
NSEQ = 4
S = 2048
D = 1024
H = 16
DH = 64
E = 8
DC = 8
TC = 16
GRP = 8
CEXP = -20.0
EPS_LN = 1e-5
EPS_G = 1e-4
WTH = 0.1
BIGNEG = -1e30

ENGS = ("tensor", "scalar", "vector", "gpsimd", "sync")


def build_program(nseq=NSEQ, taps=(), dev=False, ring_n=4, mark_n=4, merged_exp=True, do_rt=True, do_D=True, c_lvl=9):
    nc = bass.Bass("TRN2", target_bir_lowering=False, detect_race_conditions=dev)
    taps = set(taps)

    x_h = nc.dram_tensor("x", [nseq, S, D], F32, kind="ExternalInput").ap()
    xT_h = nc.dram_tensor("xT", [nseq, D, S], F32, kind="ExternalInput").ap()
    wq_h = nc.dram_tensor("wq", [D, D], F32, kind="ExternalInput").ap()
    wk_h = nc.dram_tensor("wk", [D, D], F32, kind="ExternalInput").ap()
    wv_h = nc.dram_tensor("wv", [D, D], F32, kind="ExternalInput").ap()
    wo_h = nc.dram_tensor("wo", [D, D], F32, kind="ExternalInput").ap()
    bo_h = nc.dram_tensor("bo", [1, D], F32, kind="ExternalInput").ap()
    gw_h = nc.dram_tensor("gw", [D, E], F32, kind="ExternalInput").ap()
    gb_h = nc.dram_tensor("gb", [1, E], F32, kind="ExternalInput").ap()
    ew_h = nc.dram_tensor("ew", [E, D, D], F32, kind="ExternalInput").ap()
    eb_h = nc.dram_tensor("eb", [E, D], F32, kind="ExternalInput").ap()
    cstf_h = nc.dram_tensor("cstf", [128, 2], F32, kind="ExternalInput").ap()
    idf_h = nc.dram_tensor("idf", [128, 128], F32, kind="ExternalInput").ap()
    on64_h = nc.dram_tensor("on64", [128, 64], F32, kind="ExternalInput").ap()
    onr_h = nc.dram_tensor("onr", [1, 128], F32, kind="ExternalInput").ap()
    io8_h = nc.dram_tensor("io8", [128, 64], F32, kind="ExternalInput").ap()
    out_h = nc.dram_tensor("out", [nseq, S, D], F32, kind="ExternalOutput").ap()
    tap_h = {}
    if "qkv" in taps:
        tap_h["t_q"] = nc.dram_tensor("t_q", [128, DC * S], F32, kind="ExternalOutput").ap()
        tap_h["t_k"] = nc.dram_tensor("t_k", [128, DC * S], F32, kind="ExternalOutput").ap()
        tap_h["t_v"] = nc.dram_tensor("t_v", [128, TC * D], F32, kind="ExternalOutput").ap()
    if "attn" in taps:
        tap_h["t_a"] = nc.dram_tensor("t_a", [128, DC * S], F32, kind="ExternalOutput").ap()
    if "x1" in taps:
        tap_h["t_x1"] = nc.dram_tensor("t_x1", [128, GRP * D], F32, kind="ExternalOutput").ap()
    if "comb" in taps:
        tap_h["t_cb"] = nc.dram_tensor("t_cb", [128, GRP * E], F32, kind="ExternalOutput").ap()
    if "dbg" in taps:
        tap_h["t_dx"] = nc.dram_tensor("t_dx", [128, 2048], F32, kind="ExternalOutput").ap()
        tap_h["t_dz"] = nc.dram_tensor("t_dz", [128, 2048], F32, kind="ExternalOutput").ap()
    if "rt" in taps:
        tap_h["t_rt"] = nc.dram_tensor("t_rt", [128, 64], F32, kind="ExternalOutput").ap()
        tap_h["t_sc"] = nc.dram_tensor("t_sc", [128, 16], F32, kind="ExternalOutput").ap()
        tap_h["t_gl"] = nc.dram_tensor("t_gl", [128, 8], F32, kind="ExternalOutput").ap()

    st = contextlib.ExitStack()
    with st:
        sb = lambda name, shape, dt: st.enter_context(nc.sbuf_tensor(name, shape, dt))
        ps = lambda name, shape, dt: st.enter_context(nc.psum_tensor(name, shape, dt))

        R1 = sb("R1", [128, 16384], BF16)
        R2 = sb("R2", [128, 32768], BF16)
        R3 = sb("R3", [128, 16384], BF16)
        RW = sb("RW", [128, 24576], BF16)
        PRB = sb("PRB", [128, 4096], BF16)
        DNX = sb("DNX", [128, 2048], F32)
        EB = [sb("EB0", [1, 1024], BF16),
              sb("EB1", [1, 1024], BF16)]
        ONB = sb("ONB", [1, 128], BF16)
        BOS = sb("BOS", [1, 1024], F32)
        GWS = sb("GWS", [128, 64], F32)
        GBS = sb("GBS", [1, 8], F32)
        CSTF = sb("CSTF", [128, 2], F32)
        IDF = sb("IDF", [128, 128], F32)
        ON64 = sb("ON64", [128, 64], BF16)
        ONR = sb("ONR", [1, 128], F32)
        CMB = sb("CMB", [128, 64], F32)
        SCR = sb("SCR", [128, 8], F32)
        SCS = sb("SCS", [128, 8], F32)
        JNK = sb("JNK", [128, 1024], BF16)
        SC8 = sb("SC8", [128, 64], F32)
        RTS = sb("RTS", [128, 384], F32)
        RT1 = sb("RT1", [128, 80], F32)
        IOT = sb("IOT", [128, 64], F32)

        R2f = R2[:, :].bitcast(F32)
        R3f = R3[:, :].bitcast(F32)
        PRBf = PRB[:, :].bitcast(F32)

        PS0 = ps("PS0", [128, 1024], F32)
        PS1 = ps("PS1", [128, 1024], F32)
        PS2 = ps("PS2", [128, 1024], F32)
        PS3 = ps("PS3", [128, 1024], F32)
        PSW = [PS0, PS1]

        sP = st.enter_context(nc.semaphore("sP"))
        sA = st.enter_context(nc.semaphore("sA"))
        sV = st.enter_context(nc.semaphore("sV"))
        sW = st.enter_context(nc.semaphore("sW"))
        sX = st.enter_context(nc.semaphore("sX"))
        sO = st.enter_context(nc.semaphore("sO"))
        block = st.enter_context(nc.Block())

        ops = {e: [] for e in ENGS}
        cnt = {"P": 0, "A": 0, "V": 0, "W": 0, "X": 0, "O": 0}
        sem_of = {"P": sP, "A": sA, "V": sV, "W": sW, "X": sX, "O": sO}

        def emd(e, fn):
            em(e, fn)
            em(e, lambda En: En.drain())

        def em(e, fn):
            if dev and e in ("vector", "scalar"):
                ops[e].append(lambda En, f=fn: (En.drain(), f(En))[-1])
            else:
                ops[e].append(fn)

        def wt(e, c, v):
            if v > 0:
                em(e, lambda En, s=sem_of[c], v=v: En.wait_ge(s, v))

        def bump(c, n=1):
            cnt[c] += n
            return cnt[c]

        def di(e, c):
            v = bump(c)
            em(e, lambda En, s=sem_of[c]: (En.drain(), En.sem_inc(s, 1)))
            return v

        def inc_last(e, c):
            v = bump(c)
            prev = ops[e].pop()
            em(e, lambda En, p=prev, s=sem_of[c]: p(En).then_inc(s, 1))
            return v

        def mm(psum_ap, lhsT_ap, rhs_ap, start, stop, tp=None, skip=False):
            em("tensor", lambda En, o=psum_ap, l=lhsT_ap, r=rhs_ap, st_=start, sp_=stop, tp=tp, sk=skip:
               En.matmul(o, l, r, start=st_, stop=sp_, tile_position=tp, skip_group_check=sk))

        def pe_inc():
            return inc_last("tensor", "P")

        def dma(e, c, dst, src):
            v = bump(c, 16)
            em(e, lambda En, d=dst, s=src, sm=sem_of[c]: En.dma_start(d, s).then_inc(sm, 16))
            em(e, lambda En, sm=sem_of[c], v=v: En.wait_ge(sm, v))
            return v

        dma("sync", "X", CSTF[:, :], cstf_h[:, :])
        dma("sync", "X", IDF[:, :], idf_h[:, :])
        dma("sync", "X", ONR[:, :], onr_h[:, :])
        v_eb0 = dma("gpsimd", "W", EB[0][:, :], eb_h[0:1, :])
        v_eb1 = dma("gpsimd", "W", EB[1][:, :], eb_h[1:2, :])
        dma("gpsimd", "W", ONB[:, :], onr_h[:, :])
        dma("sync", "X", BOS[:, :], bo_h[:, :])
        dma("sync", "X", GBS[:, :], gb_h[:, :])
        v_gw = dma("sync", "X", GWS[:, :].rearrange("p (k e) -> p k e", e=8),
                   gw_h.rearrange("(k p) e -> p k e", p=128))
        dma("gpsimd", "W", ON64[:, :], on64_h[:, :])
        dma("sync", "X", IOT[:, :], io8_h[:, :])
        def load_w_slot(slot, src_ap):
            return dma("gpsimd", "W", RW[:, slot * 8192:(slot + 1) * 8192]
                       .rearrange("p (k i) -> p k i", k=8),
                       src_ap.rearrange("(k p) i -> p k i", p=128))
        load_w_slot(0, wq_h)
        load_w_slot(1, wk_h)
        v_wv0 = load_w_slot(2, wv_h)
        v_xT0 = dma("gpsimd", "W", R1[:, :].rearrange("p (k t) -> p k t", k=8),
                    xT_h[0].rearrange("(k p) t -> p k t", p=128))
        v_boot_x = cnt["X"]
        for _e in ("tensor", "scalar", "vector"):
            wt(_e, "X", v_boot_x)

        negC = CSTF[:, 0:1]
        epsT = CSTF[:, 1:2]

        stv = {
            "wo_ready": 0,
            "exp_ready": [0, 0],
            "eb_ready": [v_eb0, v_eb1],
            "qkvw_ready": [0, 0, v_wv0],
            "xT_ready": v_xT0,
            "x_ready": {},
            "attnT_done": 0,
            "pvfree_v": 0, "pvfree_a": 0,
            "exp_done_a": 0,
            "a_evac": [],
            "stt_hist": [],
            "z_hist": {},
            "out_hist": {},
            "x1T_evac": {},
            "wo_used": 0,
            "v_used": 0,
            "q_used": 0, "k_used": 0,
            "r1_free": 0,
            "bias_hist": [],
        }

        def tap_dump(name, src_ap, gate_c, gate_v):
            if name in tap_h:
                wt("gpsimd", gate_c, gate_v)
                dma("gpsimd", "W", tap_h[name][:, :], src_ap)

        def phase_A(s):
            a_rot = []
            tile_idx = 0
            wt("tensor", "V", stv.get("d_psum_free", cnt["V"]))
            wt("tensor", "A", stv.get("gate_evac_last_a", 0))
            wt("tensor", "O", cnt["O"])
            wt("scalar", "V", stv.get("d_done_v", 0))

            def proj_tile(dst_ap, lhsT_of_kc, rhs_of_kc, last_q=None, last_k=None, last_v=None):
                nonlocal tile_idx
                pst = [PS0, PS1, PS2, PS3][tile_idx % 4]
                if len(a_rot) >= 4:
                    wt("tensor", "A", a_rot[tile_idx - 4])
                for kc in range(8):
                    l = lhsT_of_kc(kc)
                    for ns in range(2):
                        mm(pst[:, ns * 512:(ns + 1) * 512], l, rhs_of_kc(kc, ns),
                           start=(kc == 0), stop=(kc == 7))
                vp = pe_inc()
                wt("scalar", "P", vp)
                em("scalar", lambda En, d=dst_ap, p=pst: En.copy(d, p[:, :]))
                va = inc_last("scalar", "A")
                a_rot.append(va)
                tile_idx += 1
                return vp, va

            wt("tensor", "W", max(stv["qkvw_ready"]))
            wt("tensor", "W", stv["xT_ready"])
            for oc in range(8):
                for th in range(2):
                    vp, _ = proj_tile(
                        R2[:, oc * 2048 + th * 1024: oc * 2048 + th * 1024 + 1024],
                        lambda kc, oc=oc: RW[:, 0 * 8192 + kc * 1024 + oc * 128: 0 * 8192 + kc * 1024 + oc * 128 + 128],
                        lambda kc, ns, th=th: R1[:, kc * 2048 + th * 1024 + ns * 512: kc * 2048 + th * 1024 + ns * 512 + 512])
            stv["q_used"] = cnt["P"]
            for oc in range(8):
                for th in range(2):
                    vp, _ = proj_tile(
                        R2[:, 16384 + oc * 2048 + th * 1024: 16384 + oc * 2048 + th * 1024 + 1024],
                        lambda kc, oc=oc: RW[:, 1 * 8192 + kc * 1024 + oc * 128: 1 * 8192 + kc * 1024 + oc * 128 + 128],
                        lambda kc, ns, th=th: R1[:, kc * 2048 + th * 1024 + ns * 512: kc * 2048 + th * 1024 + ns * 512 + 512])
            stv["k_used"] = cnt["P"]
            for qt in range(16):
                vp, _ = proj_tile(
                    R3[:, qt * 1024: qt * 1024 + 1024],
                    lambda kc, qt=qt: R1[:, kc * 2048 + qt * 128: kc * 2048 + qt * 128 + 128],
                    lambda kc, ns: RW[:, 2 * 8192 + kc * 1024 + ns * 512: 2 * 8192 + kc * 1024 + ns * 512 + 512])
            stv["v_used"] = cnt["P"]
            stv["a_last_evacs"] = a_rot[-4:]
            stv["a_evacs_visible"] = di("scalar", "A")
            if s == 0:
                tap_dump("t_q", R2[:, 0:16384], "A", a_rot[15])
                tap_dump("t_k", R2[:, 16384:32768], "A", a_rot[31])
                tap_dump("t_v", R3[:, :], "A", a_rot[47])

        def phase_B(s):
            wt("gpsimd", "P", stv["v_used"])
            stv["wo_ready"] = load_w_slot(2, wo_h)
            wt("tensor", "A", stv["a_evacs_visible"])

            exp_drain = 0
            exp_half = {}
            pv_hist = []
            for h in range(H):
                c, r0 = h // 2, (h % 2) * 64
                last_read = [stv["exp_done_a"], stv["exp_done_a"]]
                last_drain = stv["exp_done_a"]
                prev_drain = None

                def emit_pv(kc):
                    b = kc % 2
                    wt("tensor", "A", prev_drain)
                    if kc == 0:
                        wt("tensor", "V", stv["pvfree_v"])
                        wt("tensor", "A", stv["pvfree_a"])
                    for half in range(2):
                        pvp = PS2 if half == 0 else PS3
                        for ns in range(2):
                            rh = PRB[:, b * 2048 + half * 1024 + ns * 512:
                                     b * 2048 + half * 1024 + ns * 512 + 512]
                            mm(pvp[0:64, ns * 512:(ns + 1) * 512],
                               R3[:, kc * 1024 + h * 64: kc * 1024 + h * 64 + 64],
                               rh, start=(kc == 0), stop=(kc == 15), tp=(0, 0), skip=True)
                    for half in range(2):
                        pvp = PS2 if half == 0 else PS3
                        for ns in range(2):
                            rh = PRB[:, b * 2048 + half * 1024 + ns * 512:
                                     b * 2048 + half * 1024 + ns * 512 + 512]
                            mm(pvp[64:128, ns * 512:(ns + 1) * 512],
                               ON64[:, :], rh, start=(kc == 0), stop=(kc == 15), tp=(0, 64), skip=True)
                    pv_hist.append(pe_inc())

                for kc in range(16):
                    b = kc % 2
                    for half in range(2):
                        pst = PSW[half]
                        wt("tensor", "A", last_read[half])
                        for ns in range(2):
                            qs = R2[r0:r0 + 64, c * 2048 + half * 1024 + ns * 512:
                                    c * 2048 + half * 1024 + ns * 512 + 512]
                            ks = R2[r0:r0 + 64, 16384 + c * 2048 + kc * 128:
                                    16384 + c * 2048 + kc * 128 + 128]
                            mm(pst[:, ns * 512:(ns + 1) * 512], ks, qs, start=True, stop=True)
                        vp = pe_inc()
                        wt("scalar", "P", vp)
                        if pv_hist:
                            wt("scalar", "P", pv_hist[-1])
                        em("scalar", lambda En, o=PRB[:, b * 2048 + half * 1024: b * 2048 + half * 1024 + 1024],
                           i=pst[:, :], bi=negC: En.activation(o, i, AF.Exp, bias=bi, scale=0.125))
                        last_read[half] = inc_last("scalar", "A")
                    prev2 = last_drain
                    last_drain = di("scalar", "A")
                    if kc > 0:
                        prev_drain = prev2
                        emit_pv(kc - 1)
                prev_drain = last_drain
                emit_pv(15)
                stv["exp_done_a"] = last_drain
                vp_pv = pv_hist[-1]
                wt("vector", "P", vp_pv)
                em("vector", lambda En, r0=r0: En.tensor_copy(DNX[r0:r0 + 64, 0:1024], PS2[64:128, :]))
                em("vector", lambda En, r0=r0: En.tensor_copy(DNX[r0:r0 + 64, 1024:2048], PS3[64:128, :]))
                stv["pvfree_v"] = di("vector", "V")
                em("vector", lambda En, r0=r0: En.reciprocal(DNX[r0:r0 + 64, 0:2048], DNX[r0:r0 + 64, 0:2048]))
                em("vector", lambda En: En.drain())
                wt("scalar", "P", vp_pv)
                em("scalar", lambda En, d=R1[r0:r0 + 64, c * 2048: c * 2048 + 1024]:
                   En.copy(d, PS2[0:64, :]))
                em("scalar", lambda En, d=R1[r0:r0 + 64, c * 2048 + 1024: c * 2048 + 2048]:
                   En.copy(d, PS3[0:64, :]))
                stv["pvfree_a"] = di("scalar", "A")
                wt("vector", "A", stv["pvfree_a"])
                em("vector", lambda En, a=R1[r0:r0 + 64, c * 2048: c * 2048 + 2048],
                   dn=DNX[r0:r0 + 64, 0:2048]: En.tensor_tensor(a, a, dn, ALU.mult))
                stv["attnT_done"] = di("vector", "V")
            if s == 0:
                tap_dump("t_a", R1[:, :], "V", stv["attnT_done"])

        def phase_C(s, g):
            wo_slot = 2 * 8192
            wt("tensor", "W", stv["wo_ready"])
            if g == 0:
                wt("tensor", "V", cnt["V"])
                wt("tensor", "A", cnt["A"])
            else:
                wt("tensor", "V", stv.get("d_psum_free", cnt["V"]))
            v_gate_evac = {-1: stv.get("gate_evac_last_a", 0)}
            chain = {}

            def wo_part(j):
                qt = g * GRP + j
                par = qt % 2
                pst = PSW[par]
                gate_v = stv["z_hist"].get(par, 0)
                if qt < 2:
                    gate_v = max(gate_v, stv["attnT_done"])
                wt("sync", "V", gate_v)
                stv["x_ready"][qt] = dma("sync", "X",
                                         DNX[:, par * 1024:(par + 1) * 1024],
                                         x_h[s, qt * 128:(qt + 1) * 128, :])
                wt("tensor", "V", stv["z_hist"].get(par, 0))
                for ic in range(8):
                    l = R1[:, ic * 2048 + qt * 128: ic * 2048 + qt * 128 + 128]
                    for ns in range(2):
                        mm(pst[:, ns * 512:(ns + 1) * 512], l,
                           RW[:, wo_slot + ic * 1024 + ns * 512: wo_slot + ic * 1024 + ns * 512 + 512],
                           start=(ic == 0), stop=False)
                for ns in range(2):
                    mm(pst[:, ns * 512:(ns + 1) * 512], ONR[0:1, :],
                       BOS[0:1, ns * 512: ns * 512 + 512], start=False, stop=True)
                vp_wo = pe_inc()
                if qt == TC - 1:
                    stv["r1_free"] = vp_wo
                zsl = R3f[:, 4096 + par * 1024: 4096 + par * 1024 + 1024]
                wt("vector", "P", vp_wo)
                wt("vector", "X", stv["x_ready"][qt])
                em("vector", lambda En, z=zsl, p=pst, xs=DNX[:, par * 1024:(par + 1) * 1024]:
                   En.scalar_tensor_tensor(z, p[:, :], 1.0, xs, ALU.mult, ALU.add,
                                           accum_out=SCR[:, 0:1]))
                stv["z_hist"][par] = inc_last("vector", "V")
                v_z = di("vector", "V")
                wt("scalar", "V", v_z)
                em("scalar", lambda En, z=zsl: En.activation(
                    JNK[:, :], z, AF.Square, accum_out=SCR[:, 1:2]))
                v_sq = di("scalar", "A")
                wt("vector", "A", v_sq)
                emd("vector", lambda En: En.tensor_scalar(SCR[:, 2:3], SCR[:, 0:1], -1.0 / D, None, ALU.mult))
                emd("vector", lambda En: En.tensor_scalar(SCR[:, 3:4], SCR[:, 2:3], SCR[:, 2:3], None, ALU.mult))
                em("vector", lambda En: En.tensor_scalar(SCR[:, 4:5], SCR[:, 1:2], 1.0 / D, SCR[:, 3:4], ALU.mult, ALU.subtract))
                v_st = di("vector", "V")
                wt("scalar", "V", v_st)
                em("scalar", lambda En: En.activation(SCR[:, 5:6], SCR[:, 4:5], AF.Ln, bias=epsT))
                em("scalar", lambda En: En.activation(SCR[:, 6:7], SCR[:, 5:6], AF.Exp, scale=-0.5))
                v_rs = di("scalar", "A")
                x1sl = R2f[:, j * 1024: j * 1024 + 1024]
                wt("vector", "A", v_rs)
                em("vector", lambda En, z=zsl, x1=x1sl: En.tensor_scalar(
                    x1, z, SCR[:, 2:3], SCR[:, 6:7], ALU.add, ALU.mult))
                chain[j] = di("vector", "V")

            def tr_part(j):
                qt = g * GRP + j
                x1sl = R2f[:, j * 1024: j * 1024 + 1024]
                wt("tensor", "V", chain[j])
                if j > 0 or g > 0 or s > 0:
                    wt("tensor", "V", stv["x1T_evac"].get("lastv", 0))
                for ic in range(8):
                    em("tensor", lambda En, o=PS3[:, ic * 128:(ic + 1) * 128],
                       i=x1sl[:, ic * 128: ic * 128 + 128]: En.transpose(o, i, IDF[:, :]))
                vp_tr = pe_inc()
                wt("vector", "P", vp_tr)
                em("vector", lambda En, d=R3[:, j * 1024: j * 1024 + 1024]:
                   En.tensor_copy(d, PS3[:, 0:1024]))
                em("vector", lambda En, j=j: En.tensor_copy(
                    PRBf[:, (j % 2) * 1024:(j % 2) * 1024 + 1024], PS3[:, 0:1024]))
                v_ev1 = di("vector", "V")
                stv["x1T_evac"][j] = v_ev1
                stv["x1T_evac"]["lastv"] = v_ev1

            def gate_part(j):
                wt("tensor", "V", stv["x1T_evac"][j])
                wt("tensor", "A", v_gate_evac.get(j - 1, 0))
                for kc in range(8):
                    mm(PS2[:, 0:8], PRBf[:, (j % 2) * 1024 + kc * 128: (j % 2) * 1024 + kc * 128 + 128],
                       GWS[:, kc * 8: kc * 8 + 8], start=(kc == 0), stop=False)
                mm(PS2[:, 0:8], ONR[0:1, :], GBS[0:1, :], start=False, stop=True)
                vp_g = pe_inc()
                wt("scalar", "P", vp_g)
                em("scalar", lambda En, d=SC8[:, j * 8: j * 8 + 8]: En.copy(d, PS2[:, 0:8]))
                v_gate_evac[j] = di("scalar", "A")
                stv["gate_evac_last_a"] = v_gate_evac[j]

            for j in range(GRP):
                wo_part(j)
                if j >= 1:
                    tr_part(j - 1)
                if j >= 2:
                    gate_part(j - 2)
            tr_part(GRP - 1)
            gate_part(GRP - 2)
            gate_part(GRP - 1)
            if not do_rt or c_lvl < 4:
                stv["comb_ready"] = cnt["V"]
                return
            T3 = lambda ap: ap.rearrange("p (s e) -> p s e", e=8)
            sl = lambda k: RTS[:, (k - 1) * 64: k * 64]
            def bc(k):
                return bass.AP(RT1[0:1, 0:1].tensor, k * 8, [[80, 128], [1, 8], [0, 8]])
            t8 = lambda k: RT1[:, k * 8:(k + 1) * 8]
            wt("vector", "A", v_gate_evac[GRP - 1])
            def rt(fn):
                em("vector", fn)
                em("vector", lambda En: En.drain())
            GL = SC8[:, 0:64]
            rt(lambda En: En.tensor_scalar(sl(1), GL, EPS_G, None, ALU.max))
            rt(lambda En: En.tensor_reduce(t8(0), T3(sl(1)), AX, ALU.max))
            rt(lambda En: En.tensor_tensor(T3(sl(2)), T3(sl(1)), bc(0), ALU.subtract))
            v_cl = di("vector", "V")
            wt("scalar", "V", v_cl)
            em("scalar", lambda En: En.activation(sl(1), sl(2), AF.Exp))
            v_ex = di("scalar", "A")
            wt("vector", "A", v_ex)
            rt(lambda En: En.tensor_reduce(t8(1), T3(sl(1)), AX, ALU.add))
            rt(lambda En: En.reciprocal(t8(2), t8(1)))
            rt(lambda En: En.tensor_tensor(T3(sl(2)), T3(sl(1)), bc(2), ALU.mult))
            rt(lambda En: En.tensor_reduce(t8(3), T3(sl(2)), AX, ALU.max))
            rt(lambda En: En.tensor_tensor(T3(sl(3)), T3(sl(2)), bc(3), ALU.is_lt))
            rt(lambda En: En.scalar_tensor_tensor(sl(3), sl(3), 8.0, IOT[:, :], ALU.mult, ALU.add))
            rt(lambda En: En.tensor_reduce(t8(4), T3(sl(3)), AX, ALU.min))
            rt(lambda En: En.tensor_tensor(T3(sl(4)), T3(IOT[:, :]), bc(4), ALU.is_equal))
            rt(lambda En: En.scalar_tensor_tensor(sl(5), sl(4), BIGNEG, sl(2), ALU.mult, ALU.add))
            rt(lambda En: En.tensor_reduce(t8(5), T3(sl(5)), AX, ALU.max))
            rt(lambda En: En.tensor_tensor(T3(sl(3)), T3(sl(5)), bc(5), ALU.is_lt))
            rt(lambda En: En.scalar_tensor_tensor(sl(3), sl(3), 8.0, IOT[:, :], ALU.mult, ALU.add))
            rt(lambda En: En.tensor_reduce(t8(6), T3(sl(3)), AX, ALU.min))
            rt(lambda En: En.tensor_tensor(T3(sl(6)), T3(IOT[:, :]), bc(6), ALU.is_equal))
            rt(lambda En: En.tensor_tensor(sl(4), sl(4), sl(6), ALU.add))
            rt(lambda En: En.tensor_scalar(sl(5), sl(2), WTH, None, ALU.max))
            rt(lambda En: En.tensor_tensor(sl(5), sl(5), sl(4), ALU.mult))
            rt(lambda En: En.tensor_reduce(t8(7), T3(sl(5)), AX, ALU.add))
            rt(lambda En: En.reciprocal(t8(8), t8(7)))
            em("vector", lambda En: En.scalar_tensor_tensor(
                T3(CMB[:, 0:64]), T3(sl(5)), 0.5, bc(8), ALU.mult, ALU.mult))
            v_rt = di("vector", "V")
            stv["comb_ready"] = v_rt
            if s == 0 and g == 0:
                tap_dump("t_x1", R2f[:, 0:8192], "V", cnt["V"])
            if s == 0 and g == 1:
                tap_dump("t_cb", CMB[:, 0:64], "V", cnt["V"])

        def phase_D(s, g):
            if not do_D:
                return
            ring = [PS0, PS1, PS2, PS3]
            marks = []
            tix = 0
            first_wait = {0: stv["z_hist"].get(g * GRP % 2, 0),
                          1: stv["z_hist"].get((g * GRP + 1) % 2, 0),
                          2: 0,
                          3: stv["x1T_evac"].get("lastv", 0)}
            def ring_tile():
                nonlocal tix
                t = ring[tix % 4]
                if tix < 4:
                    wt("tensor", "V", first_wait.get(tix % 4, 0))
                    if tix % 4 == 2:
                        wt("tensor", "A", stv.get("gate_evac_last_a", 0))
                else:
                    wt("tensor", "V", marks[tix - 4])
                tix += 1
                return t
            wt("vector", "V", stv.get("comb_ready", 0))
            v_z2 = {}
            for e in range(E):
                slot = e % 2
                wt("tensor", "W", stv["exp_ready"][slot])
                wt("tensor", "W", stv["eb_ready"][slot])
                for j in range(GRP):
                    qt = g * GRP + j
                    pst = ring_tile()
                    wt("tensor", "V", stv["x1T_evac"][j])
                    for kc in range(8):
                        l = R3[:, j * 1024 + kc * 128: j * 1024 + kc * 128 + 128]
                        for ns in range(2):
                            mm(pst[:, ns * 512:(ns + 1) * 512], l,
                               RW[:, slot * 8192 + kc * 1024 + ns * 512: slot * 8192 + kc * 1024 + ns * 512 + 512],
                               start=(kc == 0), stop=False)
                    for ns in range(2):
                        mm(pst[:, ns * 512:(ns + 1) * 512], ONB[0:1, :],
                           EB[slot][0:1, ns * 512: ns * 512 + 512], start=False, stop=True)
                    vp = pe_inc()
                    accsl = R2f[:, 8192 + j * 1024: 8192 + j * 1024 + 1024]
                    x1sl = R2f[:, j * 1024: j * 1024 + 1024]
                    wt("vector", "P", vp)
                    if e == E - 1:
                        em("vector", lambda En, a=accsl, p=pst, sc=CMB[:, j * 8 + e: j * 8 + e + 1],
                           j=j: En.scalar_tensor_tensor(a, p[:, :], sc, a, ALU.mult, ALU.add,
                                                        accum_out=SCS[:, j: j + 1]))
                    else:
                        em("vector", lambda En, a=accsl, p=pst, sc=CMB[:, j * 8 + e: j * 8 + e + 1],
                           i1=(x1sl if e == 0 else accsl):
                           En.scalar_tensor_tensor(a, p[:, :], sc, i1, ALU.mult, ALU.add))
                    marks.append(inc_last("vector", "V"))
                    if e == E - 1 and j == 3:
                        v3 = di("vector", "V")
                    elif e == E - 1 and j == GRP - 1:
                        v7 = di("vector", "V")
                _prefetch_after_expert(s, g, e)
            for j in range(GRP):
                v_z2[j] = v3 if j <= 3 else v7
            stv["d_psum_free"] = v7
            for j in range(GRP):
                qt = g * GRP + j
                accsl = R2f[:, 8192 + j * 1024: 8192 + j * 1024 + 1024]
                wt("scalar", "V", v_z2[j])
                em("scalar", lambda En, a=accsl: En.activation(
                    JNK[:, :], a, AF.Square, accum_out=SCR[:, 2:3]))
                v_sq = di("scalar", "A")
                wt("vector", "A", v_sq)
                emd("vector", lambda En, j=j: En.tensor_scalar(SCR[:, 3:4], SCS[:, j: j + 1], -1.0 / D, None, ALU.mult))
                emd("vector", lambda En: En.tensor_scalar(SCR[:, 4:5], SCR[:, 3:4], SCR[:, 3:4], None, ALU.mult))
                em("vector", lambda En: En.tensor_scalar(SCR[:, 5:6], SCR[:, 2:3], 1.0 / D, SCR[:, 4:5], ALU.mult, ALU.subtract))
                v_st = di("vector", "V")
                wt("scalar", "V", v_st)
                em("scalar", lambda En: En.activation(SCR[:, 6:7], SCR[:, 5:6], AF.Ln, bias=epsT))
                em("scalar", lambda En: En.activation(SCR[:, 7:8], SCR[:, 6:7], AF.Exp, scale=-0.5))
                v_rs = di("scalar", "A")
                par = j % 2
                osl = R3f[:, 6144 + par * 1024: 6144 + par * 1024 + 1024]
                wt("vector", "A", v_rs)
                wt("vector", "O", stv["out_hist"].get(par, 0))
                em("vector", lambda En, a=accsl, o=osl: En.tensor_scalar(
                    o, a, SCR[:, 3:4], SCR[:, 7:8], ALU.add, ALU.mult))
                v_out = di("vector", "V")
                stv["d_done_v"] = v_out
                wt("sync", "V", v_out)
                stv["out_hist"][par] = dma("sync", "O", out_h[s, qt * 128:(qt + 1) * 128, :], osl)

        def _prefetch_after_expert(s, g, e):
            vp = cnt["P"]
            slot = e % 2
            def eb_load(idx):
                stv["eb_ready"][slot] = dma("gpsimd", "W", EB[slot][:, :],
                                            eb_h[idx: idx + 1, :])
            if g == 0:
                if e < E - 2:
                    wt("gpsimd", "P", vp)
                    stv["exp_ready"][slot] = dma(
                        "gpsimd", "W", RW[:, slot * 8192:(slot + 1) * 8192]
                        .rearrange("p (k i) -> p k i", k=8),
                        ew_h[e + 2].rearrange("(k p) i -> p k i", p=128))
                    eb_load(e + 2)
                else:
                    wt("gpsimd", "P", vp)
                    stv["exp_ready"][slot] = dma(
                        "gpsimd", "W", RW[:, slot * 8192:(slot + 1) * 8192]
                        .rearrange("p (k i) -> p k i", k=8),
                        ew_h[e - 6].rearrange("(k p) i -> p k i", p=128))
                    eb_load(e - 6)
            else:
                if e < E - 2:
                    wt("gpsimd", "P", vp)
                    stv["exp_ready"][slot] = dma(
                        "gpsimd", "W", RW[:, slot * 8192:(slot + 1) * 8192]
                        .rearrange("p (k i) -> p k i", k=8),
                        ew_h[e + 2].rearrange("(k p) i -> p k i", p=128))
                    eb_load(e + 2)
                elif s < nseq - 1:
                    wt("gpsimd", "P", vp)
                    stv["qkvw_ready"][slot] = load_w_slot(slot, wq_h if slot == 0 else wk_h)
                    eb_load(e - 6)

        for s in range(nseq):
            phase_A(s)
            phase_B(s)
            phase_C(s, 0)
            wt("gpsimd", "P", stv["k_used"])
            stv["exp_ready"][0] = dma("gpsimd", "W",
                                      RW[:, 0:8192].rearrange("p (k i) -> p k i", k=8),
                                      ew_h[0].rearrange("(k p) i -> p k i", p=128))
            stv["exp_ready"][1] = dma("gpsimd", "W",
                                      RW[:, 8192:16384].rearrange("p (k i) -> p k i", k=8),
                                      ew_h[1].rearrange("(k p) i -> p k i", p=128))
            phase_D(s, 0)
            phase_C(s, 1)
            if s < nseq - 1:
                wt("gpsimd", "P", stv["r1_free"])
                stv["xT_ready"] = dma("gpsimd", "W",
                                      R1[:, :].rearrange("p (k t) -> p k t", k=8),
                                      xT_h[s + 1].rearrange("(k p) t -> p k t", p=128))
                stv["qkvw_ready"][2] = load_w_slot(2, wv_h)
                stv["x_ready"] = {}
            phase_D(s, 1)

        final_o = cnt["O"]
        wt("sync", "O", final_o)
        final_w = cnt["W"]
        wt("gpsimd", "W", final_w)

        @block.tensor
        def _(En):
            for f in ops["tensor"]:
                f(En)

        @block.scalar
        def _(En):
            for f in ops["scalar"]:
                f(En)

        @block.vector
        def _(En):
            for f in ops["vector"]:
                f(En)

        @block.gpsimd
        def _(En):
            for f in ops["gpsimd"]:
                f(En)

        @block.sync
        def _(En):
            for f in ops["sync"]:
                f(En)

    return nc


def _consts():
    cstf = np.zeros((128, 2), np.float32)
    cstf[:, 0] = CEXP
    cstf[:, 1] = EPS_LN
    idf = np.eye(128, dtype=np.float32)
    on64 = np.ones((128, 64), np.float32)
    onr = np.ones((1, 128), np.float32)
    io8 = np.tile(np.arange(8, dtype=np.float32), (128, 8))
    return cstf, idf, on64, onr, io8


def kernel(L0_attn_hidden_states, L0_attn_Wq, L0_attn_Wkv, L0_attn_Wo, L0_attn_bo,
           L0_ffn_gate_W, L0_ffn_gate_b, L0_ffn_expert_W, L0_ffn_expert_b):
    x = np.asarray(L0_attn_hidden_states, np.float32)
    wq = np.ascontiguousarray(np.asarray(L0_attn_Wq, np.float32))
    wkv = np.asarray(L0_attn_Wkv, np.float32)
    wk = np.ascontiguousarray(wkv[:, :D])
    wv = np.ascontiguousarray(wkv[:, D:])
    wo = np.ascontiguousarray(np.asarray(L0_attn_Wo, np.float32))
    bo = np.asarray(L0_attn_bo, np.float32).reshape(1, D)
    gw = np.ascontiguousarray(np.asarray(L0_ffn_gate_W, np.float32))
    gb = np.asarray(L0_ffn_gate_b, np.float32).reshape(1, E)
    ew = np.ascontiguousarray(np.asarray(L0_ffn_expert_W, np.float32))
    eb = np.ascontiguousarray(np.asarray(L0_ffn_expert_b, np.float32))
    cstf, idf, on64, onr, io8 = _consts()

    nc = build_program(NSEQ)
    in_maps = []
    for c in range(NCORES):
        xs = np.ascontiguousarray(x[c * NSEQ:(c + 1) * NSEQ])
        xTs = np.ascontiguousarray(xs.transpose(0, 2, 1))
        in_maps.append({
            "x": xs, "xT": xTs, "wq": wq, "wk": wk, "wv": wv, "wo": wo, "bo": bo,
            "gw": gw, "gb": gb, "ew": ew, "eb": eb,
            "cstf": cstf, "idf": idf, "on64": on64, "onr": onr, "io8": io8,
        })
    res = run_bass_kernel_spmd(nc, in_maps, core_ids=list(range(NCORES)))
    out = np.concatenate([r["out"] for r in res.results], axis=0)
    return out


# version 18
# speedup vs baseline: 1.0548x; 1.0548x over previous
import contextlib
import numpy as np
import ml_dtypes

import concourse.bass as bass
import concourse.mybir as mybir
from concourse.bass_utils import run_bass_kernel_spmd
from concourse.alu_op_type import AluOpType as ALU

F32 = mybir.dt.float32
BF16 = mybir.dt.bfloat16
AF = mybir.ActivationFunctionType
AX = mybir.AxisListType.X
BF = ml_dtypes.bfloat16

NCORES = 8
NSEQ = 4
S = 2048
D = 1024
H = 16
DH = 64
E = 8
DC = 8
TC = 16
GRP = 8
CEXP = -20.0
EPS_LN = 1e-5
EPS_G = 1e-4
WTH = 0.1
BIGNEG = -1e30

ENGS = ("tensor", "scalar", "vector", "gpsimd", "sync")


def build_program(nseq=NSEQ, taps=(), dev=False, ring_n=4, mark_n=4, merged_exp=True, do_rt=True, do_D=True, c_lvl=9):
    nc = bass.Bass("TRN2", target_bir_lowering=False, detect_race_conditions=dev)
    taps = set(taps)

    x_h = nc.dram_tensor("x", [nseq, S, D], F32, kind="ExternalInput").ap()
    xT_h = nc.dram_tensor("xT", [nseq, D, S], F32, kind="ExternalInput").ap()
    wq_h = nc.dram_tensor("wq", [D, D], F32, kind="ExternalInput").ap()
    wk_h = nc.dram_tensor("wk", [D, D], F32, kind="ExternalInput").ap()
    wv_h = nc.dram_tensor("wv", [D, D], F32, kind="ExternalInput").ap()
    wo_h = nc.dram_tensor("wo", [D, D], F32, kind="ExternalInput").ap()
    bo_h = nc.dram_tensor("bo", [1, D], F32, kind="ExternalInput").ap()
    gw_h = nc.dram_tensor("gw", [D, E], F32, kind="ExternalInput").ap()
    gb_h = nc.dram_tensor("gb", [1, E], F32, kind="ExternalInput").ap()
    ew_h = nc.dram_tensor("ew", [E, D, D], F32, kind="ExternalInput").ap()
    eb_h = nc.dram_tensor("eb", [E, D], F32, kind="ExternalInput").ap()
    cstf_h = nc.dram_tensor("cstf", [128, 2], F32, kind="ExternalInput").ap()
    idf_h = nc.dram_tensor("idf", [128, 128], F32, kind="ExternalInput").ap()
    on64_h = nc.dram_tensor("on64", [128, 64], F32, kind="ExternalInput").ap()
    onr_h = nc.dram_tensor("onr", [1, 128], F32, kind="ExternalInput").ap()
    io8_h = nc.dram_tensor("io8", [128, 64], F32, kind="ExternalInput").ap()
    out_h = nc.dram_tensor("out", [nseq, S, D], F32, kind="ExternalOutput").ap()
    tap_h = {}
    if "qkv" in taps:
        tap_h["t_q"] = nc.dram_tensor("t_q", [128, DC * S], F32, kind="ExternalOutput").ap()
        tap_h["t_k"] = nc.dram_tensor("t_k", [128, DC * S], F32, kind="ExternalOutput").ap()
        tap_h["t_v"] = nc.dram_tensor("t_v", [128, TC * D], F32, kind="ExternalOutput").ap()
    if "attn" in taps:
        tap_h["t_a"] = nc.dram_tensor("t_a", [128, DC * S], F32, kind="ExternalOutput").ap()
    if "x1" in taps:
        tap_h["t_x1"] = nc.dram_tensor("t_x1", [128, GRP * D], F32, kind="ExternalOutput").ap()
    if "comb" in taps:
        tap_h["t_cb"] = nc.dram_tensor("t_cb", [128, GRP * E], F32, kind="ExternalOutput").ap()
    if "dbg" in taps:
        tap_h["t_dx"] = nc.dram_tensor("t_dx", [128, 2048], F32, kind="ExternalOutput").ap()
        tap_h["t_dz"] = nc.dram_tensor("t_dz", [128, 2048], F32, kind="ExternalOutput").ap()
    if "rt" in taps:
        tap_h["t_rt"] = nc.dram_tensor("t_rt", [128, 64], F32, kind="ExternalOutput").ap()
        tap_h["t_sc"] = nc.dram_tensor("t_sc", [128, 16], F32, kind="ExternalOutput").ap()
        tap_h["t_gl"] = nc.dram_tensor("t_gl", [128, 8], F32, kind="ExternalOutput").ap()

    st = contextlib.ExitStack()
    with st:
        sb = lambda name, shape, dt: st.enter_context(nc.sbuf_tensor(name, shape, dt))
        ps = lambda name, shape, dt: st.enter_context(nc.psum_tensor(name, shape, dt))

        R1 = sb("R1", [128, 16384], BF16)
        R2 = sb("R2", [128, 32768], BF16)
        R3 = sb("R3", [128, 16384], BF16)
        RW = sb("RW", [128, 24576], BF16)
        PRB = sb("PRB", [128, 4096], BF16)
        DNX = sb("DNX", [128, 2048], F32)
        EBS = sb("EBS", [8, 1024], BF16)
        BOS = sb("BOS", [1, 1024], F32)
        GWS = sb("GWS", [128, 64], F32)
        GBS = sb("GBS", [1, 8], F32)
        CSTF = sb("CSTF", [128, 2], F32)
        IDF = sb("IDF", [128, 128], F32)
        ON64 = sb("ON64", [128, 64], BF16)
        ONR = sb("ONR", [1, 128], F32)
        CMB = sb("CMB", [128, 64], F32)
        CMT = sb("CMT", [8, 1024], BF16)
        SCR = sb("SCR", [128, 8], F32)
        SCS = sb("SCS", [128, 8], F32)
        JNK = sb("JNK", [128, 1024], BF16)
        SC8 = sb("SC8", [128, 64], F32)
        RTS = sb("RTS", [128, 384], F32)
        RT1 = sb("RT1", [128, 80], F32)
        IOT = sb("IOT", [128, 64], F32)

        R2f = R2[:, :].bitcast(F32)
        R3f = R3[:, :].bitcast(F32)
        PRBf = PRB[:, :].bitcast(F32)

        PS0 = ps("PS0", [128, 1024], F32)
        PS1 = ps("PS1", [128, 1024], F32)
        PS2 = ps("PS2", [128, 1024], F32)
        PS3 = ps("PS3", [128, 1024], F32)
        PSW = [PS0, PS1]

        sP = st.enter_context(nc.semaphore("sP"))
        sA = st.enter_context(nc.semaphore("sA"))
        sV = st.enter_context(nc.semaphore("sV"))
        sW = st.enter_context(nc.semaphore("sW"))
        sX = st.enter_context(nc.semaphore("sX"))
        sO = st.enter_context(nc.semaphore("sO"))
        block = st.enter_context(nc.Block())

        ops = {e: [] for e in ENGS}
        cnt = {"P": 0, "A": 0, "V": 0, "W": 0, "X": 0, "O": 0}
        sem_of = {"P": sP, "A": sA, "V": sV, "W": sW, "X": sX, "O": sO}

        def emd(e, fn):
            em(e, fn)
            em(e, lambda En: En.drain())

        def em(e, fn):
            if dev and e in ("vector", "scalar"):
                ops[e].append(lambda En, f=fn: (En.drain(), f(En))[-1])
            else:
                ops[e].append(fn)

        def wt(e, c, v):
            if v > 0:
                em(e, lambda En, s=sem_of[c], v=v: En.wait_ge(s, v))

        def bump(c, n=1):
            cnt[c] += n
            return cnt[c]

        def di(e, c):
            v = bump(c)
            em(e, lambda En, s=sem_of[c]: (En.drain(), En.sem_inc(s, 1)))
            return v

        def inc_last(e, c):
            v = bump(c)
            prev = ops[e].pop()
            em(e, lambda En, p=prev, s=sem_of[c]: p(En).then_inc(s, 1))
            return v

        def mm(psum_ap, lhsT_ap, rhs_ap, start, stop, tp=None, skip=False):
            em("tensor", lambda En, o=psum_ap, l=lhsT_ap, r=rhs_ap, st_=start, sp_=stop, tp=tp, sk=skip:
               En.matmul(o, l, r, start=st_, stop=sp_, tile_position=tp, skip_group_check=sk))

        def pe_inc():
            return inc_last("tensor", "P")

        def dma(e, c, dst, src):
            v = bump(c, 16)
            em(e, lambda En, d=dst, s=src, sm=sem_of[c]: En.dma_start(d, s).then_inc(sm, 16))
            em(e, lambda En, sm=sem_of[c], v=v: En.wait_ge(sm, v))
            return v

        dma("sync", "X", CSTF[:, :], cstf_h[:, :])
        dma("sync", "X", IDF[:, :], idf_h[:, :])
        dma("sync", "X", ONR[:, :], onr_h[:, :])
        dma("gpsimd", "W", EBS[:, :], eb_h[:, :])
        dma("sync", "X", BOS[:, :], bo_h[:, :])
        dma("sync", "X", GBS[:, :], gb_h[:, :])
        v_gw = dma("sync", "X", GWS[:, :].rearrange("p (k e) -> p k e", e=8),
                   gw_h.rearrange("(k p) e -> p k e", p=128))
        dma("gpsimd", "W", ON64[:, :], on64_h[:, :])
        dma("sync", "X", IOT[:, :], io8_h[:, :])
        def load_w_slot(slot, src_ap):
            return dma("gpsimd", "W", RW[:, slot * 8192:(slot + 1) * 8192]
                       .rearrange("p (k i) -> p k i", k=8),
                       src_ap.rearrange("(k p) i -> p k i", p=128))
        load_w_slot(0, wq_h)
        load_w_slot(1, wk_h)
        v_wv0 = load_w_slot(2, wv_h)
        v_xT0 = dma("gpsimd", "W", R1[:, :].rearrange("p (k t) -> p k t", k=8),
                    xT_h[0].rearrange("(k p) t -> p k t", p=128))
        v_boot_x = cnt["X"]
        for _e in ("tensor", "scalar", "vector"):
            wt(_e, "X", v_boot_x)

        negC = CSTF[:, 0:1]
        epsT = CSTF[:, 1:2]

        stv = {
            "wo_ready": 0,
            "exp_ready": [0, 0],
            "qkvw_ready": [0, 0, v_wv0],
            "xT_ready": v_xT0,
            "x_ready": {},
            "attnT_done": 0,
            "pvfree_v": 0, "pvfree_a": 0,
            "exp_done_a": 0,
            "a_evac": [],
            "stt_hist": [],
            "z_hist": {},
            "out_hist": {},
            "x1T_evac": {},
            "wo_used": 0,
            "v_used": 0,
            "q_used": 0, "k_used": 0,
            "r1_free": 0,
            "bias_hist": [],
        }

        def tap_dump(name, src_ap, gate_c, gate_v):
            if name in tap_h:
                wt("gpsimd", gate_c, gate_v)
                dma("gpsimd", "W", tap_h[name][:, :], src_ap)

        def phase_A(s):
            a_rot = []
            tile_idx = 0
            wt("tensor", "V", stv.get("d_psum_free", cnt["V"]))
            wt("tensor", "A", stv.get("gate_evac_last_a", 0))
            wt("tensor", "O", cnt["O"])
            wt("scalar", "V", stv.get("d_done_v", 0))

            def proj_tile(dst_ap, lhsT_of_kc, rhs_of_kc, last_q=None, last_k=None, last_v=None):
                nonlocal tile_idx
                pst = [PS0, PS1, PS2, PS3][tile_idx % 4]
                if len(a_rot) >= 4:
                    wt("tensor", "A", a_rot[tile_idx - 4])
                for kc in range(8):
                    l = lhsT_of_kc(kc)
                    for ns in range(2):
                        mm(pst[:, ns * 512:(ns + 1) * 512], l, rhs_of_kc(kc, ns),
                           start=(kc == 0), stop=(kc == 7))
                vp = pe_inc()
                wt("scalar", "P", vp)
                em("scalar", lambda En, d=dst_ap, p=pst: En.copy(d, p[:, :]))
                va = inc_last("scalar", "A")
                a_rot.append(va)
                tile_idx += 1
                return vp, va

            wt("tensor", "W", max(stv["qkvw_ready"]))
            wt("tensor", "W", stv["xT_ready"])
            for oc in range(8):
                for th in range(2):
                    vp, _ = proj_tile(
                        R2[:, oc * 2048 + th * 1024: oc * 2048 + th * 1024 + 1024],
                        lambda kc, oc=oc: RW[:, 0 * 8192 + kc * 1024 + oc * 128: 0 * 8192 + kc * 1024 + oc * 128 + 128],
                        lambda kc, ns, th=th: R1[:, kc * 2048 + th * 1024 + ns * 512: kc * 2048 + th * 1024 + ns * 512 + 512])
            stv["q_used"] = cnt["P"]
            for oc in range(8):
                for th in range(2):
                    vp, _ = proj_tile(
                        R2[:, 16384 + oc * 2048 + th * 1024: 16384 + oc * 2048 + th * 1024 + 1024],
                        lambda kc, oc=oc: RW[:, 1 * 8192 + kc * 1024 + oc * 128: 1 * 8192 + kc * 1024 + oc * 128 + 128],
                        lambda kc, ns, th=th: R1[:, kc * 2048 + th * 1024 + ns * 512: kc * 2048 + th * 1024 + ns * 512 + 512])
            stv["k_used"] = cnt["P"]
            for qt in range(16):
                vp, _ = proj_tile(
                    R3[:, qt * 1024: qt * 1024 + 1024],
                    lambda kc, qt=qt: R1[:, kc * 2048 + qt * 128: kc * 2048 + qt * 128 + 128],
                    lambda kc, ns: RW[:, 2 * 8192 + kc * 1024 + ns * 512: 2 * 8192 + kc * 1024 + ns * 512 + 512])
            stv["v_used"] = cnt["P"]
            stv["a_last_evacs"] = a_rot[-4:]
            stv["a_evacs_visible"] = di("scalar", "A")
            if s == 0:
                tap_dump("t_q", R2[:, 0:16384], "A", a_rot[15])
                tap_dump("t_k", R2[:, 16384:32768], "A", a_rot[31])
                tap_dump("t_v", R3[:, :], "A", a_rot[47])

        def phase_B(s):
            wt("gpsimd", "P", stv["v_used"])
            stv["wo_ready"] = load_w_slot(2, wo_h)
            wt("tensor", "A", stv["a_evacs_visible"])

            exp_drain = 0
            exp_half = {}
            pv_hist = []
            for h in range(H):
                c, r0 = h // 2, (h % 2) * 64
                last_read = [stv["exp_done_a"], stv["exp_done_a"]]
                last_drain = stv["exp_done_a"]
                prev_drain = None

                def emit_pv(kc):
                    b = kc % 2
                    wt("tensor", "A", prev_drain)
                    if kc == 0:
                        wt("tensor", "V", stv["pvfree_v"])
                        wt("tensor", "A", stv["pvfree_a"])
                    for half in range(2):
                        pvp = PS2 if half == 0 else PS3
                        for ns in range(2):
                            rh = PRB[:, b * 2048 + half * 1024 + ns * 512:
                                     b * 2048 + half * 1024 + ns * 512 + 512]
                            mm(pvp[0:64, ns * 512:(ns + 1) * 512],
                               R3[:, kc * 1024 + h * 64: kc * 1024 + h * 64 + 64],
                               rh, start=(kc == 0), stop=(kc == 15), tp=(0, 0), skip=True)
                    for half in range(2):
                        pvp = PS2 if half == 0 else PS3
                        for ns in range(2):
                            rh = PRB[:, b * 2048 + half * 1024 + ns * 512:
                                     b * 2048 + half * 1024 + ns * 512 + 512]
                            mm(pvp[64:128, ns * 512:(ns + 1) * 512],
                               ON64[:, :], rh, start=(kc == 0), stop=(kc == 15), tp=(0, 64), skip=True)
                    pv_hist.append(pe_inc())

                for kc in range(16):
                    b = kc % 2
                    for half in range(2):
                        pst = PSW[half]
                        wt("tensor", "A", last_read[half])
                        for ns in range(2):
                            qs = R2[r0:r0 + 64, c * 2048 + half * 1024 + ns * 512:
                                    c * 2048 + half * 1024 + ns * 512 + 512]
                            ks = R2[r0:r0 + 64, 16384 + c * 2048 + kc * 128:
                                    16384 + c * 2048 + kc * 128 + 128]
                            mm(pst[:, ns * 512:(ns + 1) * 512], ks, qs, start=True, stop=True)
                        vp = pe_inc()
                        wt("scalar", "P", vp)
                        if pv_hist:
                            wt("scalar", "P", pv_hist[-1])
                        em("scalar", lambda En, o=PRB[:, b * 2048 + half * 1024: b * 2048 + half * 1024 + 1024],
                           i=pst[:, :], bi=negC: En.activation(o, i, AF.Exp, bias=bi, scale=0.125))
                        last_read[half] = inc_last("scalar", "A")
                    prev2 = last_drain
                    last_drain = di("scalar", "A")
                    if kc > 0:
                        prev_drain = prev2
                        emit_pv(kc - 1)
                prev_drain = last_drain
                emit_pv(15)
                stv["exp_done_a"] = last_drain
                vp_pv = pv_hist[-1]
                wt("vector", "P", vp_pv)
                em("vector", lambda En, r0=r0: En.tensor_copy(DNX[r0:r0 + 64, 0:1024], PS2[64:128, :]))
                em("vector", lambda En, r0=r0: En.tensor_copy(DNX[r0:r0 + 64, 1024:2048], PS3[64:128, :]))
                stv["pvfree_v"] = di("vector", "V")
                if h < H - 1:
                    em("vector", lambda En, r0=r0: En.reciprocal(DNX[r0:r0 + 64, 0:2048], DNX[r0:r0 + 64, 0:2048]))
                    em("vector", lambda En: En.drain())
                wt("scalar", "P", vp_pv)
                em("scalar", lambda En, d=R1[r0:r0 + 64, c * 2048: c * 2048 + 1024]:
                   En.copy(d, PS2[0:64, :]))
                em("scalar", lambda En, d=R1[r0:r0 + 64, c * 2048 + 1024: c * 2048 + 2048]:
                   En.copy(d, PS3[0:64, :]))
                stv["pvfree_a"] = di("scalar", "A")
                if h == H - 1:
                    ro = 64 - r0
                    wt("scalar", "V", stv["pvfree_v"])
                    em("scalar", lambda En, r0=r0, ro=ro: En.activation(
                        DNX[ro:ro + 64, 0:2048], DNX[r0:r0 + 64, 0:2048], AF.Ln))
                    em("scalar", lambda En, r0=r0, ro=ro: En.activation(
                        DNX[r0:r0 + 64, 0:2048], DNX[ro:ro + 64, 0:2048], AF.Exp, scale=-1.0))
                    stv["pvfree_a"] = di("scalar", "A")
                wt("vector", "A", stv["pvfree_a"])
                em("vector", lambda En, a=R1[r0:r0 + 64, c * 2048: c * 2048 + 2048],
                   dn=DNX[r0:r0 + 64, 0:2048]: En.tensor_tensor(a, a, dn, ALU.mult))
                stv["attnT_done"] = di("vector", "V")
            if s == 0:
                tap_dump("t_a", R1[:, :], "V", stv["attnT_done"])

        def phase_C(s, g):
            wo_slot = 2 * 8192
            wt("tensor", "W", stv["wo_ready"])
            if g == 0:
                wt("tensor", "V", cnt["V"])
                wt("tensor", "A", cnt["A"])
            else:
                wt("tensor", "V", stv.get("d_psum_free", cnt["V"]))
            v_gate_evac = {-1: stv.get("gate_evac_last_a", 0)}
            chain = {}

            def wo_part(j):
                qt = g * GRP + j
                par = qt % 2
                pst = PSW[par]
                gate_v = stv["z_hist"].get(par, 0)
                if qt < 2:
                    gate_v = max(gate_v, stv["attnT_done"])
                wt("sync", "V", gate_v)
                stv["x_ready"][qt] = dma("sync", "X",
                                         DNX[:, par * 1024:(par + 1) * 1024],
                                         x_h[s, qt * 128:(qt + 1) * 128, :])
                wt("tensor", "V", stv["z_hist"].get(par, 0))
                for ic in range(8):
                    l = R1[:, ic * 2048 + qt * 128: ic * 2048 + qt * 128 + 128]
                    for ns in range(2):
                        mm(pst[:, ns * 512:(ns + 1) * 512], l,
                           RW[:, wo_slot + ic * 1024 + ns * 512: wo_slot + ic * 1024 + ns * 512 + 512],
                           start=(ic == 0), stop=False)
                for ns in range(2):
                    mm(pst[:, ns * 512:(ns + 1) * 512], ONR[0:1, :],
                       BOS[0:1, ns * 512: ns * 512 + 512], start=False, stop=True)
                vp_wo = pe_inc()
                if qt == TC - 1:
                    stv["r1_free"] = vp_wo
                zsl = R3f[:, 4096 + par * 1024: 4096 + par * 1024 + 1024]
                wt("vector", "P", vp_wo)
                wt("vector", "X", stv["x_ready"][qt])
                em("vector", lambda En, z=zsl, p=pst, xs=DNX[:, par * 1024:(par + 1) * 1024]:
                   En.scalar_tensor_tensor(z, p[:, :], 1.0, xs, ALU.mult, ALU.add,
                                           accum_out=SCR[:, 0:1]))
                stv["z_hist"][par] = inc_last("vector", "V")
                v_z = di("vector", "V")
                wt("scalar", "V", v_z)
                em("scalar", lambda En, z=zsl: En.activation(
                    JNK[:, :], z, AF.Square, accum_out=SCR[:, 1:2]))
                v_sq = di("scalar", "A")
                wt("vector", "A", v_sq)
                emd("vector", lambda En: En.tensor_scalar(SCR[:, 2:3], SCR[:, 0:1], -1.0 / D, None, ALU.mult))
                emd("vector", lambda En: En.tensor_scalar(SCR[:, 3:4], SCR[:, 2:3], SCR[:, 2:3], None, ALU.mult))
                em("vector", lambda En: En.tensor_scalar(SCR[:, 4:5], SCR[:, 1:2], 1.0 / D, SCR[:, 3:4], ALU.mult, ALU.subtract))
                v_st = di("vector", "V")
                wt("scalar", "V", v_st)
                em("scalar", lambda En: En.activation(SCR[:, 5:6], SCR[:, 4:5], AF.Ln, bias=epsT))
                em("scalar", lambda En: En.activation(SCR[:, 6:7], SCR[:, 5:6], AF.Exp, scale=-0.5))
                v_rs = di("scalar", "A")
                x1sl = R2f[:, j * 1024: j * 1024 + 1024]
                wt("vector", "A", v_rs)
                em("vector", lambda En, z=zsl, x1=x1sl: En.tensor_scalar(
                    x1, z, SCR[:, 2:3], SCR[:, 6:7], ALU.add, ALU.mult))
                chain[j] = di("vector", "V")

            def tr_part(j):
                qt = g * GRP + j
                x1sl = R2f[:, j * 1024: j * 1024 + 1024]
                wt("tensor", "V", chain[j])
                if j > 0 or g > 0 or s > 0:
                    wt("tensor", "V", stv["x1T_evac"].get("lastv", 0))
                for ic in range(8):
                    em("tensor", lambda En, o=PS3[:, ic * 128:(ic + 1) * 128],
                       i=x1sl[:, ic * 128: ic * 128 + 128]: En.transpose(o, i, IDF[:, :]))
                vp_tr = pe_inc()
                wt("vector", "P", vp_tr)
                em("vector", lambda En, d=R3[:, j * 1024: j * 1024 + 1024]:
                   En.tensor_copy(d, PS3[:, 0:1024]))
                em("vector", lambda En, j=j: En.tensor_copy(
                    PRBf[:, (j % 2) * 1024:(j % 2) * 1024 + 1024], PS3[:, 0:1024]))
                v_ev1 = di("vector", "V")
                stv["x1T_evac"][j] = v_ev1
                stv["x1T_evac"]["lastv"] = v_ev1

            def gate_part(j):
                wt("tensor", "V", stv["x1T_evac"][j])
                wt("tensor", "A", v_gate_evac.get(j - 1, 0))
                for kc in range(8):
                    mm(PS2[:, 0:8], PRBf[:, (j % 2) * 1024 + kc * 128: (j % 2) * 1024 + kc * 128 + 128],
                       GWS[:, kc * 8: kc * 8 + 8], start=(kc == 0), stop=False)
                mm(PS2[:, 0:8], ONR[0:1, :], GBS[0:1, :], start=False, stop=True)
                vp_g = pe_inc()
                wt("scalar", "P", vp_g)
                em("scalar", lambda En, d=SC8[:, j * 8: j * 8 + 8]: En.copy(d, PS2[:, 0:8]))
                v_gate_evac[j] = di("scalar", "A")
                stv["gate_evac_last_a"] = v_gate_evac[j]

            for j in range(GRP):
                wo_part(j)
                if j >= 1:
                    tr_part(j - 1)
                if j >= 2:
                    gate_part(j - 2)
            tr_part(GRP - 1)
            gate_part(GRP - 2)
            gate_part(GRP - 1)
            if not do_rt or c_lvl < 4:
                stv["comb_ready"] = cnt["V"]
                stv["cmt_ready"] = cnt["V"]
                return
            T3 = lambda ap: ap.rearrange("p (s e) -> p s e", e=8)
            sl = lambda k: RTS[:, (k - 1) * 64: k * 64]
            def bc(k):
                return bass.AP(RT1[0:1, 0:1].tensor, k * 8, [[80, 128], [1, 8], [0, 8]])
            t8 = lambda k: RT1[:, k * 8:(k + 1) * 8]
            wt("vector", "A", v_gate_evac[GRP - 1])
            def rt(fn):
                em("vector", fn)
                em("vector", lambda En: En.drain())
            GL = SC8[:, 0:64]
            rt(lambda En: En.tensor_scalar(sl(1), GL, EPS_G, None, ALU.max))
            rt(lambda En: En.tensor_reduce(t8(0), T3(sl(1)), AX, ALU.max))
            rt(lambda En: En.tensor_tensor(T3(sl(2)), T3(sl(1)), bc(0), ALU.subtract))
            v_cl = di("vector", "V")
            wt("scalar", "V", v_cl)
            em("scalar", lambda En: En.activation(sl(1), sl(2), AF.Exp))
            v_ex = di("scalar", "A")
            wt("vector", "A", v_ex)
            rt(lambda En: En.tensor_reduce(t8(1), T3(sl(1)), AX, ALU.add))
            rt(lambda En: En.reciprocal(t8(2), t8(1)))
            rt(lambda En: En.tensor_tensor(T3(sl(2)), T3(sl(1)), bc(2), ALU.mult))
            rt(lambda En: En.tensor_reduce(t8(3), T3(sl(2)), AX, ALU.max))
            rt(lambda En: En.tensor_tensor(T3(sl(3)), T3(sl(2)), bc(3), ALU.is_lt))
            rt(lambda En: En.scalar_tensor_tensor(sl(3), sl(3), 8.0, IOT[:, :], ALU.mult, ALU.add))
            rt(lambda En: En.tensor_reduce(t8(4), T3(sl(3)), AX, ALU.min))
            rt(lambda En: En.tensor_tensor(T3(sl(4)), T3(IOT[:, :]), bc(4), ALU.is_equal))
            rt(lambda En: En.scalar_tensor_tensor(sl(5), sl(4), BIGNEG, sl(2), ALU.mult, ALU.add))
            rt(lambda En: En.tensor_reduce(t8(5), T3(sl(5)), AX, ALU.max))
            rt(lambda En: En.tensor_tensor(T3(sl(3)), T3(sl(5)), bc(5), ALU.is_lt))
            rt(lambda En: En.scalar_tensor_tensor(sl(3), sl(3), 8.0, IOT[:, :], ALU.mult, ALU.add))
            rt(lambda En: En.tensor_reduce(t8(6), T3(sl(3)), AX, ALU.min))
            rt(lambda En: En.tensor_tensor(T3(sl(6)), T3(IOT[:, :]), bc(6), ALU.is_equal))
            rt(lambda En: En.tensor_tensor(sl(4), sl(4), sl(6), ALU.add))
            rt(lambda En: En.tensor_scalar(sl(5), sl(2), WTH, None, ALU.max))
            rt(lambda En: En.tensor_tensor(sl(5), sl(5), sl(4), ALU.mult))
            rt(lambda En: En.tensor_reduce(t8(7), T3(sl(5)), AX, ALU.add))
            rt(lambda En: En.reciprocal(t8(8), t8(7)))
            em("vector", lambda En: En.scalar_tensor_tensor(
                T3(CMB[:, 0:64]), T3(sl(5)), 0.5, bc(8), ALU.mult, ALU.mult))
            v_rt = di("vector", "V")
            stv["comb_ready"] = v_rt
            if s == 0 and g == 0:
                tap_dump("t_x1", R2f[:, 0:8192], "V", cnt["V"])
            if s == 0 and g == 1:
                tap_dump("t_cb", CMB[:, 0:64], "V", cnt["V"])

        def phase_D(s, g):
            if not do_D:
                return
            ring = [PS0, PS1, PS2, PS3]
            marks = []
            tix = 0
            first_wait = {0: stv["z_hist"].get(g * GRP % 2, 0),
                          1: stv["z_hist"].get((g * GRP + 1) % 2, 0),
                          2: 0,
                          3: stv["x1T_evac"].get("lastv", 0)}
            def ring_tile():
                nonlocal tix
                t = ring[tix % 4]
                if tix < 4:
                    wt("tensor", "V", first_wait.get(tix % 4, 0))
                    if tix % 4 == 2:
                        wt("tensor", "A", stv.get("gate_evac_last_a", 0))
                else:
                    wt("tensor", "V", marks[tix - 4])
                tix += 1
                return t
            wt("vector", "V", stv.get("comb_ready", 0))
            for e in range(E):
                slot = e % 2
                wt("tensor", "W", stv["exp_ready"][slot])
                for j in range(GRP):
                    qt = g * GRP + j
                    pst = ring_tile()
                    wt("tensor", "V", stv["x1T_evac"][j])
                    for kc in range(8):
                        l = R3[:, j * 1024 + kc * 128: j * 1024 + kc * 128 + 128]
                        for ns in range(2):
                            mm(pst[:, ns * 512:(ns + 1) * 512], l,
                               RW[:, slot * 8192 + kc * 1024 + ns * 512: slot * 8192 + kc * 1024 + ns * 512 + 512],
                               start=(kc == 0), stop=(kc == 7))
                    vp = pe_inc()
                    accsl = R2f[:, 8192 + j * 1024: 8192 + j * 1024 + 1024]
                    x1sl = R2f[:, j * 1024: j * 1024 + 1024]
                    wt("vector", "P", vp)
                    em("vector", lambda En, a=accsl, p=pst, sc=CMB[:, j * 8 + e: j * 8 + e + 1],
                       i1=(x1sl if e == 0 else accsl):
                       En.scalar_tensor_tensor(a, p[:, :], sc, i1, ALU.mult, ALU.add))
                    marks.append(inc_last("vector", "V"))
                _prefetch_after_expert(s, g, e)
            wt("tensor", "V", stv["comb_ready"])
            wt("tensor", "A", stv.get("gate_evac_last_a", 0))
            wt("tensor", "V", marks[-2])
            for j in range(GRP):
                em("tensor", lambda En, o=PS2[0:8, j * 128:(j + 1) * 128],
                   i=CMB[:, j * 8: j * 8 + 8]: En.transpose(o, i, IDF[:, :]))
            vp_ct = pe_inc()
            wt("vector", "P", vp_ct)
            em("vector", lambda En: En.tensor_copy(CMT[0:8, 0:1024], PS2[0:8, 0:1024]))
            stv["cmt_ready"] = di("vector", "V")
            v_z2 = {}
            for j in range(GRP):
                pbs = ring_tile()
                wt("tensor", "V", stv["cmt_ready"])
                for ns in range(2):
                    mm(pbs[:, ns * 512:(ns + 1) * 512], CMT[0:8, j * 128: j * 128 + 128],
                       EBS[0:8, ns * 512: ns * 512 + 512], start=True, stop=True)
                vp_b = pe_inc()
                accsl = R2f[:, 8192 + j * 1024: 8192 + j * 1024 + 1024]
                wt("vector", "P", vp_b)
                em("vector", lambda En, a=accsl, p=pbs, j=j: En.scalar_tensor_tensor(
                    a, p[:, :], 1.0, a, ALU.mult, ALU.add, accum_out=SCS[:, j: j + 1]))
                marks.append(inc_last("vector", "V"))
                if j == 3:
                    v3 = di("vector", "V")
                elif j == GRP - 1:
                    v7 = di("vector", "V")
            for j in range(GRP):
                v_z2[j] = v3 if j <= 3 else v7
            stv["d_psum_free"] = v7
            for j in range(GRP):
                qt = g * GRP + j
                accsl = R2f[:, 8192 + j * 1024: 8192 + j * 1024 + 1024]
                wt("scalar", "V", v_z2[j])
                em("scalar", lambda En, a=accsl: En.activation(
                    JNK[:, :], a, AF.Square, accum_out=SCR[:, 2:3]))
                v_sq = di("scalar", "A")
                wt("vector", "A", v_sq)
                emd("vector", lambda En, j=j: En.tensor_scalar(SCR[:, 3:4], SCS[:, j: j + 1], -1.0 / D, None, ALU.mult))
                emd("vector", lambda En: En.tensor_scalar(SCR[:, 4:5], SCR[:, 3:4], SCR[:, 3:4], None, ALU.mult))
                em("vector", lambda En: En.tensor_scalar(SCR[:, 5:6], SCR[:, 2:3], 1.0 / D, SCR[:, 4:5], ALU.mult, ALU.subtract))
                v_st = di("vector", "V")
                wt("scalar", "V", v_st)
                em("scalar", lambda En: En.activation(SCR[:, 6:7], SCR[:, 5:6], AF.Ln, bias=epsT))
                em("scalar", lambda En: En.activation(SCR[:, 7:8], SCR[:, 6:7], AF.Exp, scale=-0.5))
                v_rs = di("scalar", "A")
                par = j % 2
                osl = R3f[:, 6144 + par * 1024: 6144 + par * 1024 + 1024]
                wt("vector", "A", v_rs)
                wt("vector", "O", stv["out_hist"].get(par, 0))
                em("vector", lambda En, a=accsl, o=osl: En.tensor_scalar(
                    o, a, SCR[:, 3:4], SCR[:, 7:8], ALU.add, ALU.mult))
                v_out = di("vector", "V")
                stv["d_done_v"] = v_out
                wt("sync", "V", v_out)
                stv["out_hist"][par] = dma("sync", "O", out_h[s, qt * 128:(qt + 1) * 128, :], osl)

        def _prefetch_after_expert(s, g, e):
            vp = cnt["P"]
            slot = e % 2
            if g == 0:
                if e < E - 2:
                    wt("gpsimd", "P", vp)
                    stv["exp_ready"][slot] = dma(
                        "gpsimd", "W", RW[:, slot * 8192:(slot + 1) * 8192]
                        .rearrange("p (k i) -> p k i", k=8),
                        ew_h[e + 2].rearrange("(k p) i -> p k i", p=128))
                else:
                    wt("gpsimd", "P", vp)
                    stv["exp_ready"][slot] = dma(
                        "gpsimd", "W", RW[:, slot * 8192:(slot + 1) * 8192]
                        .rearrange("p (k i) -> p k i", k=8),
                        ew_h[e - 6].rearrange("(k p) i -> p k i", p=128))
            else:
                if e < E - 2:
                    wt("gpsimd", "P", vp)
                    stv["exp_ready"][slot] = dma(
                        "gpsimd", "W", RW[:, slot * 8192:(slot + 1) * 8192]
                        .rearrange("p (k i) -> p k i", k=8),
                        ew_h[e + 2].rearrange("(k p) i -> p k i", p=128))
                elif s < nseq - 1:
                    wt("gpsimd", "P", vp)
                    stv["qkvw_ready"][slot] = load_w_slot(slot, wq_h if slot == 0 else wk_h)

        for s in range(nseq):
            phase_A(s)
            phase_B(s)
            phase_C(s, 0)
            wt("gpsimd", "P", stv["k_used"])
            stv["exp_ready"][0] = dma("gpsimd", "W",
                                      RW[:, 0:8192].rearrange("p (k i) -> p k i", k=8),
                                      ew_h[0].rearrange("(k p) i -> p k i", p=128))
            stv["exp_ready"][1] = dma("gpsimd", "W",
                                      RW[:, 8192:16384].rearrange("p (k i) -> p k i", k=8),
                                      ew_h[1].rearrange("(k p) i -> p k i", p=128))
            phase_D(s, 0)
            phase_C(s, 1)
            if s < nseq - 1:
                wt("gpsimd", "P", stv["r1_free"])
                stv["xT_ready"] = dma("gpsimd", "W",
                                      R1[:, :].rearrange("p (k t) -> p k t", k=8),
                                      xT_h[s + 1].rearrange("(k p) t -> p k t", p=128))
                stv["qkvw_ready"][2] = load_w_slot(2, wv_h)
                stv["x_ready"] = {}
            phase_D(s, 1)

        final_o = cnt["O"]
        wt("sync", "O", final_o)
        final_w = cnt["W"]
        wt("gpsimd", "W", final_w)

        @block.tensor
        def _(En):
            for f in ops["tensor"]:
                f(En)

        @block.scalar
        def _(En):
            for f in ops["scalar"]:
                f(En)

        @block.vector
        def _(En):
            for f in ops["vector"]:
                f(En)

        @block.gpsimd
        def _(En):
            for f in ops["gpsimd"]:
                f(En)

        @block.sync
        def _(En):
            for f in ops["sync"]:
                f(En)

    return nc


def _consts():
    cstf = np.zeros((128, 2), np.float32)
    cstf[:, 0] = CEXP
    cstf[:, 1] = EPS_LN
    idf = np.eye(128, dtype=np.float32)
    on64 = np.ones((128, 64), np.float32)
    onr = np.ones((1, 128), np.float32)
    io8 = np.tile(np.arange(8, dtype=np.float32), (128, 8))
    return cstf, idf, on64, onr, io8


def kernel(L0_attn_hidden_states, L0_attn_Wq, L0_attn_Wkv, L0_attn_Wo, L0_attn_bo,
           L0_ffn_gate_W, L0_ffn_gate_b, L0_ffn_expert_W, L0_ffn_expert_b):
    x = np.asarray(L0_attn_hidden_states, np.float32)
    wq = np.ascontiguousarray(np.asarray(L0_attn_Wq, np.float32))
    wkv = np.asarray(L0_attn_Wkv, np.float32)
    wk = np.ascontiguousarray(wkv[:, :D])
    wv = np.ascontiguousarray(wkv[:, D:])
    wo = np.ascontiguousarray(np.asarray(L0_attn_Wo, np.float32))
    bo = np.asarray(L0_attn_bo, np.float32).reshape(1, D)
    gw = np.ascontiguousarray(np.asarray(L0_ffn_gate_W, np.float32))
    gb = np.asarray(L0_ffn_gate_b, np.float32).reshape(1, E)
    ew = np.ascontiguousarray(np.asarray(L0_ffn_expert_W, np.float32))
    eb = np.ascontiguousarray(np.asarray(L0_ffn_expert_b, np.float32))
    cstf, idf, on64, onr, io8 = _consts()

    nc = build_program(NSEQ)
    in_maps = []
    for c in range(NCORES):
        xs = np.ascontiguousarray(x[c * NSEQ:(c + 1) * NSEQ])
        xTs = np.ascontiguousarray(xs.transpose(0, 2, 1))
        in_maps.append({
            "x": xs, "xT": xTs, "wq": wq, "wk": wk, "wv": wv, "wo": wo, "bo": bo,
            "gw": gw, "gb": gb, "ew": ew, "eb": eb,
            "cstf": cstf, "idf": idf, "on64": on64, "onr": onr, "io8": io8,
        })
    res = run_bass_kernel_spmd(nc, in_maps, core_ids=list(range(NCORES)))
    out = np.concatenate([r["out"] for r in res.results], axis=0)
    return out
